# version 12
# baseline (speedup 1.0000x reference)
import numpy as np
from contextlib import ExitStack

import concourse.bass as bass
import concourse.mybir as mybir
from concourse.bass_utils import run_bass_kernel_spmd

F32 = mybir.dt.float32
BF16 = mybir.dt.bfloat16
I32 = mybir.dt.int32
AF = mybir.ActivationFunctionType
ALU = mybir.AluOpType

D = 1024
HID = 3072
NHB = HID // 128
NDC = D // 128
GROUP = 256
NT = GROUP // 128
EPS = 1e-5
SQ2I = 0.7071067811865476
INV127 = 1.0 / 127.0
INV127_3 = INV127 ** 3


class Prog:

    ENGINES = ("pe", "dv", "sc", "sy", "gp")

    def __init__(self):
        self.ops = {e: [] for e in self.ENGINES}
        self.cnt = {"pe": 0, "dv": 0, "sc": 0,
                    "swk": 0, "sw1": 0, "sw2": 0, "swg": 0, "swu": 0, "swd": 0,
                    "sidf": 0, "sxa": 0, "sxb": 0, "sxta": 0, "sxtb": 0,
                    "soa": 0, "sob": 0}
        self.ev = {}

    def add(self, eng, fn, waits=(), event=None, sem=None, n=1):
        if sem is None:
            sem = {"pe": "pe", "dv": "dv", "sc": "sc"}.get(eng)
        w = [name for name in waits if name is not None]
        self.ops[eng].append((fn, w, sem, n))
        if sem is not None:
            self.cnt[sem] += n
            if event is not None:
                self.ev[event] = (sem, self.cnt[sem])
        else:
            assert event is None


def build_core(rows=4096):
    assert rows % GROUP == 0
    ngroups = rows // GROUP
    nc = bass.Bass("TRN2", target_bir_lowering=False)

    x_d = nc.dram_tensor("x", [rows, D], F32, kind="ExternalInput").ap()
    xT_d = nc.dram_tensor("xT", [D, rows], F32, kind="ExternalInput").ap()
    wkT_d = nc.dram_tensor("wkT", [D, D], F32, kind="ExternalInput").ap()
    w1T_d = nc.dram_tensor("w1T", [D, 64], F32, kind="ExternalInput").ap()
    w2T_d = nc.dram_tensor("w2T", [64, D], F32, kind="ExternalInput").ap()
    wgT_d = nc.dram_tensor("wgT", [D, HID], I32, kind="ExternalInput").ap()
    wuT_d = nc.dram_tensor("wuT", [D, HID], I32, kind="ExternalInput").ap()
    wdT_d = nc.dram_tensor("wdT", [HID, D], I32, kind="ExternalInput").ap()
    idf_d = nc.dram_tensor("idf", [128, 128], F32, kind="ExternalInput").ap()
    out_d = nc.dram_tensor("out", [rows, D], F32, kind="ExternalOutput").ap()

    p = Prog()

    with ExitStack() as ctx:
        e = ctx.enter_context
        WKT = e(nc.sbuf_tensor("WKT", [128, NDC * D], BF16))
        W1T = e(nc.sbuf_tensor("W1T", [128, NDC * 64], BF16))
        W2T = e(nc.sbuf_tensor("W2T", [128, D], BF16))
        WGT = e(nc.sbuf_tensor("WGT", [128, NDC * HID], BF16))
        WUT = e(nc.sbuf_tensor("WUT", [128, NDC * HID], BF16))
        WDT = e(nc.sbuf_tensor("WDT", [128, NHB * D], BF16))
        IDF = e(nc.sbuf_tensor("IDF", [128, 128], F32))
        xTb = e(nc.sbuf_tensor("xTb", [128, 2 * NDC * GROUP], BF16))
        kTs = e(nc.sbuf_tensor("kTs", [128, NDC * GROUP], BF16))
        x1T = e(nc.sbuf_tensor("x1T", [128, NDC * GROUP], BF16))
        r1 = e(nc.sbuf_tensor("r1", [128, 2 * D], F32))
        x1 = e(nc.sbuf_tensor("x1", [128, 4 * D], F32))
        fft = e(nc.sbuf_tensor("fft", [128, 6 * GROUP], BF16))
        aT = fft.ap()[:, 0:GROUP]
        sgb = fft.ap()[:, 0:2 * GROUP]
        gsb = fft.ap()[:, 2 * GROUP:4 * GROUP]
        hbf = fft.ap()[:, 4 * GROUP:6 * GROUP]
        dmp = fft.ap()[:, 0:D]
        st_ = e(nc.sbuf_tensor("st_", [128, 66], F32))
        pA = e(nc.psum_tensor("pA", [128, 1024], F32))
        pB = e(nc.psum_tensor("pB", [128, 1024], F32))
        pa0 = e(nc.psum_tensor("pa0", [128, 1024], F32))
        pa1 = e(nc.psum_tensor("pa1", [128, 1024], F32))
        pa = [pa0, pa1]
        pstA = [pA, pB]
        pkslc = [pA.ap()[:, 0:GROUP], pA.ap()[:, 512:512 + GROUP]]
        pg = [pA.ap()[:, 0:GROUP], pB.ap()[:, 0:GROUP]]
        pu = [pA.ap()[:, 512:512 + GROUP], pB.ap()[:, 512:512 + GROUP]]
        sems = {}
        for sname in ("pe", "dv", "sc", "swk", "sw1", "sw2", "swg", "swu",
                      "swd", "sidf", "sxa", "sxb", "sxta", "sxtb", "soa", "sob"):
            sems[sname] = e(nc.semaphore(sname))

        stats_col = {}

        def xsl(T):
            return x1.ap()[:, (T % 4) * D:((T % 4) + 1) * D]

        def stcol(g, t, ln, k):
            return (g % 2) * 32 + ln * 16 + k * 2 + t


        def ln_stats_pair(g, ln, sum_evs, sq_evs, extra=(), after_m=None):
            b0 = (g % 2) * 32 + ln * 16
            tag = f"P{ln}_{g}"
            p.add("dv", lambda dv, b0=b0: dv.tensor_scalar_mul(
                st_[:, b0 + 4:b0 + 6], st_[:, b0 + 0:b0 + 2], 1.0 / D),
                waits=list(sum_evs) + list(extra), event=f"m{tag}")
            if after_m is not None:
                after_m()
            p.add("dv", lambda dv, b0=b0: dv.tensor_scalar(
                st_[:, b0 + 6:b0 + 8], st_[:, b0 + 2:b0 + 4], 1.0 / D, EPS,
                ALU.mult, ALU.add),
                waits=list(sq_evs), event=f"q{tag}")
            p.add("dv", lambda dv, b0=b0: dv.tensor_tensor(
                st_[:, b0 + 8:b0 + 10], st_[:, b0 + 4:b0 + 6], st_[:, b0 + 4:b0 + 6],
                ALU.mult),
                waits=[f"m{tag}"], event=f"mm{tag}")
            p.add("dv", lambda dv, b0=b0: dv.tensor_tensor(
                st_[:, b0 + 10:b0 + 12], st_[:, b0 + 6:b0 + 8], st_[:, b0 + 8:b0 + 10],
                ALU.subtract),
                waits=[f"q{tag}", f"mm{tag}"], event=f"tv{tag}")
            p.add("dv", lambda dv, b0=b0: dv.reciprocal(
                st_[:, b0 + 12:b0 + 14], st_[:, b0 + 10:b0 + 12]),
                waits=[f"tv{tag}"], event=f"rc{tag}")
            p.add("sc", lambda sc, b0=b0: sc.activation(
                st_[:, b0 + 14:b0 + 16], st_[:, b0 + 12:b0 + 14], AF.Sqrt),
                waits=[f"rc{tag}"], event=f"rstd{tag}")
            return f"rstd{tag}", f"m{tag}", b0 + 4, b0 + 14

        def emit_ln2s(gq, t):
            c0 = stcol(gq, t, 1, 0)
            w = [f"dn_{gq}_{NHB - 1}", f"nrm1_{gq}_{t}"]
            sl = xsl(2 * gq + t)

            def r2stt(dv, sl=sl, t=t, c0=c0):
                return dv.scalar_tensor_tensor(
                    sl, pa[t][:, :], INV127_3, sl,
                    ALU.mult, ALU.add,
                    accum_out=st_[:, c0:c0 + 1])
            p.add("dv", r2stt, waits=w, event=f"ln2s{gq}_{t}")

        def emit_sq2(gq, t):
            c0 = stcol(gq, t, 1, 1)

            def sq2(sc, t=t, c0=c0):
                return sc.activation(dmp, xsl(2 * gq + t), AF.Square,
                                     accum_out=st_[:, c0:c0 + 1])
            p.add("sc", sq2, waits=[f"ln2s{gq}_{t}", last_sq[0]],
                  event=f"sq2_{gq}_{t}")
            last_sq[0] = f"sq2_{gq}_{t}"

        def emit_ln2_nrm(gq, t, rstd_ev, m_ev, m_col, rs_col, r1_free_ev):
            T = 2 * gq + t
            dst = r1.ap()[:, t * D:(t + 1) * D]

            def nrm2(dv, t=t, dst=dst, m_col=m_col, rs_col=rs_col):
                return dv.tensor_scalar(
                    dst, xsl(2 * gq + t),
                    st_[:, m_col + t:m_col + t + 1], st_[:, rs_col + t:rs_col + t + 1],
                    ALU.subtract, ALU.mult)
            p.add("dv", nrm2,
                  waits=[rstd_ev, m_ev, f"ln2s{gq}_{t}", r1_free_ev],
                  event=f"nrm2_{gq}_{t}")
            p.add("sy", lambda sy, T=T, dst=dst: sy.dma_start(
                out=out_d[T * 128:(T + 1) * 128, :], in_=dst),
                waits=[f"nrm2_{gq}_{t}"], sem=("soa" if T % 2 == 0 else "sob"),
                n=16, event=f"out{T}")

        def dma_cast(eng_ap_out, in_ap):
            def fn(gp):
                return gp.dma_start(out=eng_ap_out, in_=in_ap)
            return fn

        wkT3 = wkT_d.rearrange("(c p) k -> c p k", p=128)
        for c in range(NDC):
            p.add("gp", dma_cast(WKT.ap()[:, c * D:(c + 1) * D], wkT3[c]),
                  sem="swk", n=16, event=f"w_wk{c}")
        p.ev["w_wk"] = p.ev[f"w_wk{NDC - 1}"]
        w1T3 = w1T_d.rearrange("(c p) m -> c p m", p=128)
        for c in range(NDC):
            p.add("gp", dma_cast(W1T.ap()[:, c * 64:(c + 1) * 64], w1T3[c]),
                  sem="sw1", n=16, event=f"w_w1{c}")
        p.ev["w_w1"] = p.ev[f"w_w1{NDC - 1}"]
        p.add("gp", dma_cast(W2T.ap()[0:64, :], w2T_d), sem="sw2", n=16, event="w_w2raw")
        xT3 = xT_d.rearrange("(c p) r -> c p r", p=128)

        def load_xT(g, waits):
            sem = "sxta" if g % 2 == 0 else "sxtb"
            half = (g % 2) * NDC * GROUP
            for c in range(NDC):
                in_ap = xT3[c][:, g * GROUP:(g + 1) * GROUP]
                out_ap = xTb.ap()[:, half + c * GROUP: half + (c + 1) * GROUP]
                p.add("gp", dma_cast(out_ap, in_ap), waits=(waits if c == 0 else ()),
                      sem=sem, n=16, event=f"xT{g}_{c}")

        load_xT(0, [])
        load_xT(1, [])
        wgT3 = wgT_d.rearrange("(c p) h -> c p h", p=128)
        wuT3 = wuT_d.rearrange("(c p) h -> c p h", p=128)
        for c in range(NDC):
            p.add("gp", dma_cast(WGT.ap()[:, c * HID:(c + 1) * HID], wgT3[c]),
                  sem="swg", n=16, event=f"w_wg{c}")
        for c in range(NDC):
            p.add("gp", dma_cast(WUT.ap()[:, c * HID:(c + 1) * HID], wuT3[c]),
                  sem="swu", n=16, event=f"w_wu{c}")
        wdT3 = wdT_d.rearrange("(c p) d -> c p d", p=128)
        for c in range(NHB):
            p.add("gp", dma_cast(WDT.ap()[:, c * D:(c + 1) * D], wdT3[c]),
                  sem="swd", n=16, event=f"w_wd{c}")
        p.ev["w_w2"] = p.ev["w_w2raw"]
        p.add("sy", lambda sy: sy.dma_start(out=IDF[:], in_=idf_d),
              sem="sidf", n=16, event="w_idf")


        def load_x(T):
            buf = r1.ap()[:, (T % 2) * D:((T % 2) + 1) * D]
            if T - 4 >= 0:
                w = [f"out{T - 4}"]
            elif T >= 2:
                w = [f"nrm1_0_{T - 2}", f"sq1_0_{T - 2}"]
            else:
                w = []
            p.add("sy", lambda sy, b=buf, T=T: sy.dma_start(
                out=b, in_=x_d[T * 128:(T + 1) * 128, :]),
                waits=w, sem=("sxa" if T % 2 == 0 else "sxb"), n=16,
                event=f"x{T}")

        load_x(0)
        load_x(1)

        def emit_K(gk):
            xb = xTb.ap()[:, (gk % 2) * NDC * GROUP:(gk % 2 + 1) * NDC * GROUP]
            xt_waits = [f"xT{gk}_{c}" for c in range(NDC)] + ["w_wk"]
            for kc in range(NDC):
                bank = pkslc[kc % 2]
                waits = list(xt_waits) if kc == 0 else []
                if kc >= 2:
                    waits.append(f"kev{gk}_{kc - 2}")
                elif kc == 0 and gk >= 1:
                    waits.append(f"r1_{gk - 1}_0")
                if gk == 1 and kc < 2:
                    waits += [f"kev0_{6 + kc}"]

                def kmm(pe, kc=kc, bank=bank, xb=xb):
                    ins = None
                    for hc in range(NDC):
                        ins = pe.matmul(
                            bank,
                            WKT[:, hc * D + kc * 128: hc * D + kc * 128 + 128],
                            xb[:, hc * GROUP:(hc + 1) * GROUP],
                            start=(hc == 0), stop=(hc == NDC - 1))
                    return ins
                p.add("pe", kmm, waits=waits, event=f"kT{gk}_{kc}")

        def emit_kev(gk, kcs, eng):
            for kc in kcs:
                src = pkslc[kc % 2]
                dst = kTs.ap()[:, kc * GROUP:(kc + 1) * GROUP]
                ewaits = [f"kT{gk}_{kc}"]
                if eng == "sc":
                    p.add("sc", lambda sc, s=src, d=dst: sc.copy(d, s),
                          waits=ewaits, event=f"kev{gk}_{kc}")
                else:
                    p.add("dv", lambda dv, s=src, d=dst: dv.tensor_copy(d, s),
                          waits=ewaits, event=f"kev{gk}_{kc}")

        emit_K(0)
        emit_kev(0, range(NDC), "dv")

        last_sq = [None]
        last_dmy = [None]

        def emit_dummy(func, name):
            def fn(sc):
                return sc.activation(st_[:, 64:65], IDF[:, 0:1], func)
            p.add("sc", fn, waits=[w for w in ("w_idf", last_dmy[0]) if w],
                  event=name)
            last_dmy[0] = name

        for g in range(ngroups):
            h1_waits = [f"kev{g}_{kc}" for kc in range(NDC)] + ["w_w1"]
            if g > 0:
                h1_waits += [f"t1_{g - 1}_{NHB - 1}", f"sg_{g - 1}_{NHB - 1}"]

            def h1mm(pe):
                ins = None
                for kc in range(NDC):
                    ins = pe.matmul(
                        pB[0:64, 0:GROUP],
                        W1T[:, kc * 64:(kc + 1) * 64],
                        kTs[:, kc * GROUP:(kc + 1) * GROUP],
                        start=(kc == 0), stop=(kc == NDC - 1))
                return ins
            p.add("pe", h1mm, waits=h1_waits, event=f"h1{g}")

            aT_waits = [f"h1{g}"]
            if g > 0:
                aT_waits += [f"ao{g - 1}_1", f"t1_{g - 1}_{NHB - 2}"]
            p.add("sc", lambda sc: sc.activation(aT[0:64, :],
                                                 pB[0:64, 0:GROUP], AF.Gelu),
                  waits=aT_waits, event=f"aT{g}")
            emit_dummy(AF.Sqrt, f"dmy_sqrt{g}")

            for t in range(NT):
                waits = [f"aT{g}", "w_w2"]
                if t == 0:
                    waits += [f"kev{g}_6", f"kev{g}_7"]
                elif g > 0:
                    waits += [f"t2_{g - 1}_{NHB - 1}", f"sg_{g - 1}_{NHB - 1}"]

                def aomm(pe, t=t):
                    ins = None
                    for sl in range(2):
                        ins = pe.matmul(
                            pstA[t][:, sl * 512:(sl + 1) * 512],
                            aT[0:64, t * 128:(t + 1) * 128],
                            W2T[0:64, sl * 512:(sl + 1) * 512],
                            start=True, stop=True)
                    return ins
                p.add("pe", aomm, waits=waits, event=f"ao{g}_{t}")

            if g + 1 < ngroups:
                emit_K(g + 1)

            def r1stt_fn(t, T, c0):
                def f(dv):
                    half = r1.ap()[:, (T % 2) * D:((T % 2) + 1) * D]
                    return dv.scalar_tensor_tensor(
                        half, pstA[t][:, :], 1.0, half,
                        ALU.mult, ALU.add,
                        accum_out=st_[:, c0:c0 + 1])
                return f

            T0, T1 = 2 * g, 2 * g + 1
            p.add("dv", r1stt_fn(0, T0, stcol(g, 0, 0, 0)),
                  waits=[f"ao{g}_0", f"x{T0}"], event=f"r1_{g}_0")
            p.add("dv", r1stt_fn(1, T1, stcol(g, 1, 0, 0)),
                  waits=[f"ao{g}_1", f"x{T1}"], event=f"r1_{g}_1")

            def sq1_fn(t, c0):
                def f(sc):
                    return sc.activation(dmp, r1[:, t * D:(t + 1) * D], AF.Square,
                                         accum_out=st_[:, c0:c0 + 1])
                return f
            sq1w = [f"r1_{g}_0", f"ao{g}_1", last_sq[0]]
            if g > 0:
                sq1w.append(f"t2_{g - 1}_{NHB - 1}")
            p.add("sc", sq1_fn(0, stcol(g, 0, 0, 1)), waits=sq1w, event=f"sq1_{g}_0")
            p.add("sc", sq1_fn(1, stcol(g, 1, 0, 1)),
                  waits=[f"r1_{g}_1", f"sq1_{g}_0"], event=f"sq1_{g}_1")
            last_sq[0] = f"sq1_{g}_1"

            def kev_after_m():
                if g + 1 < ngroups:
                    emit_kev(g + 1, [0, 1], "dv")
            if g + 1 < ngroups:
                emit_kev(g + 1, [2, 3, 4], "sc")
            rstd1_ev, m1_ev, m1c, rs1c = ln_stats_pair(
                g, 0, [f"r1_{g}_0", f"r1_{g}_1"], [f"sq1_{g}_0", f"sq1_{g}_1"],
                after_m=kev_after_m)

            def nrm1_fn(t, T, m_col, rs_col):
                def f(dv):
                    return dv.tensor_scalar(
                        xsl(T), r1[:, t * D:(t + 1) * D],
                        st_[:, m_col + t:m_col + t + 1],
                        st_[:, rs_col + t:rs_col + t + 1],
                        ALU.subtract, ALU.mult)
                return f
            nw0 = [rstd1_ev, m1_ev, f"r1_{g}_0", f"sq1_{g}_0"]
            if g >= 2:
                nw0.append(f"nrm2_{g - 2}_0")
            p.add("dv", nrm1_fn(0, T0, m1c, rs1c), waits=nw0, event=f"nrm1_{g}_0")
            nw1 = [rstd1_ev, f"r1_{g}_1", f"sq1_{g}_1"]
            if g >= 2:
                nw1.append(f"nrm2_{g - 2}_1")
            p.add("dv", nrm1_fn(1, T1, m1c, rs1c), waits=nw1, event=f"nrm1_{g}_1")
            if g + 1 < ngroups:
                emit_kev(g + 1, [5, 6, 7], "sc")

            if g > 0:
                emit_ln2s(g - 1, 0)
                emit_ln2s(g - 1, 1)
                emit_sq2(g - 1, 0)
                emit_sq2(g - 1, 1)
                rstd2_ev, m2_ev, m2c, rs2c = ln_stats_pair(
                    g - 1, 1, [f"ln2s{g - 1}_0", f"ln2s{g - 1}_1"],
                    [f"sq2_{g - 1}_0", f"sq2_{g - 1}_1"])
                emit_ln2_nrm(g - 1, 0, rstd2_ev, m2_ev, m2c, rs2c, f"nrm1_{g}_0")
                emit_ln2_nrm(g - 1, 1, rstd2_ev, m2_ev, m2c, rs2c, f"nrm1_{g}_1")

            emit_dummy(AF.Sigmoid, f"dmy_sig{g}")

            for t in range(NT):
                for rd in range(2):
                    tw = [f"nrm1_{g}_{t}", "w_idf"]
                    if t == 0 and rd == 0:
                        tw.append(f"r1_{g}_1")
                    else:
                        tw.append(f"xev{g}_{t}_{rd - 1}" if rd == 1 else f"xev{g}_{t - 1}_1")

                    def xtp(pe, t=t, rd=rd, T=2 * g + t):
                        base = (T % 4) * D
                        ins = None
                        for q in range(4):
                            dc = rd * 4 + q
                            ins = pe.transpose(
                                pB[:, 512 + q * 128: 512 + (q + 1) * 128],
                                x1[:, base + dc * 128: base + dc * 128 + 128],
                                IDF[:])
                        return ins
                    p.add("pe", xtp, waits=tw, event=f"xtp{g}_{t}_{rd}")

                    def xev(eng, t=t, rd=rd):
                        src = pB.ap()[:, 512:1024].rearrange("p (q j) -> p q j", j=128)
                        dst = x1T.ap().rearrange("p (c r) -> p c r", r=GROUP)[
                            :, rd * 4:(rd + 1) * 4, t * 128:(t + 1) * 128]
                        if rd == 0:
                            return eng.copy(dst, src)
                        return eng.tensor_copy(dst, src)
                    p.add("sc" if rd == 0 else "dv", xev, waits=[f"xtp{g}_{t}_{rd}"],
                          event=f"xev{g}_{t}_{rd}")

            x1T_ready = [f"xev{g}_{t}_{rd}" for t in range(NT) for rd in range(2)]

            if g + 1 < ngroups:
                load_x(2 * (g + 1))
                load_x(2 * (g + 1) + 1)
            if g + 2 < ngroups:
                load_xT(g + 2, [f"kT{g}_{NDC - 1}"])

            wg_waits = [f"w_wg{c}" for c in range(NDC)]
            wu_waits = [f"w_wu{c}" for c in range(NDC)]

            def emit_down(hb):
                par = hb % 2
                dw = [f"t2_{g}_{hb}", f"w_wd{NHB - 1}"]
                if hb == 0 and g > 0:
                    dw.append(f"ln2s{g - 1}_{NT - 1}")

                def dnf(pe, hb=hb, par=par):
                    ins = None
                    for t in range(NT):
                        for sl in range(2):
                            ins = pe.matmul(
                                pa[t][:, sl * 512:(sl + 1) * 512],
                                hbf[:, par * GROUP + t * 128: par * GROUP + t * 128 + 128],
                                WDT[:, hb * D + sl * 512: hb * D + sl * 512 + 512],
                                start=(hb == 0), stop=(hb == NHB - 1))
                    return ins
                p.add("pe", dnf, waits=dw, event=f"dn_{g}_{hb}")

            for hb in range(NHB):
                par = hb % 2
                gw = list(x1T_ready)
                if g == 0 and hb < 2:
                    gw += wg_waits
                if hb >= 2:
                    gw += [f"sg_{g}_{hb - 2}", f"t1_{g}_{hb - 2}"]
                if hb == 0 and g + 1 < ngroups:
                    gw.append(f"kev{g + 1}_6")

                def gmm(pe, hb=hb, par=par):
                    ins = None
                    for dc in range(NDC):
                        ins = pe.matmul(
                            pg[par],
                            WGT[:, dc * HID + hb * 128: dc * HID + hb * 128 + 128],
                            x1T[:, dc * GROUP:(dc + 1) * GROUP],
                            start=(dc == 0), stop=(dc == NDC - 1))
                    return ins
                p.add("pe", gmm, waits=gw, event=f"g_{g}_{hb}")

                uw = []
                if g == 0 and hb < 2:
                    uw += wu_waits
                if hb >= 2:
                    uw.append(f"t2_{g}_{hb - 2}")
                if hb == 0 and g + 1 < ngroups:
                    uw.append(f"kev{g + 1}_7")

                def umm(pe, hb=hb, par=par):
                    ins = None
                    for dc in range(NDC):
                        ins = pe.matmul(
                            pu[par],
                            WUT[:, dc * HID + hb * 128: dc * HID + hb * 128 + 128],
                            x1T[:, dc * GROUP:(dc + 1) * GROUP],
                            start=(dc == 0), stop=(dc == NDC - 1))
                    return ins
                p.add("pe", umm, waits=uw, event=f"u_{g}_{hb}")

                if hb >= 1:
                    emit_down(hb - 1)

                sgw = [f"g_{g}_{hb}"]
                if hb >= 2:
                    sgw.append(f"t1_{g}_{hb - 2}")
                elif hb == 0:
                    sgw.append(last_sq[0])

                def sgf(sc, hb=hb, par=par):
                    return sc.activation(sgb[:, par * GROUP:(par + 1) * GROUP],
                                         pg[par], AF.Sigmoid, scale=INV127)
                p.add("sc", sgf, waits=sgw, event=f"sg_{g}_{hb}")

                def t1f(dv, hb=hb, par=par):
                    return dv.tensor_tensor(gsb[:, par * GROUP:(par + 1) * GROUP],
                                            pg[par],
                                            sgb[:, par * GROUP:(par + 1) * GROUP],
                                            ALU.mult)
                p.add("dv", t1f, waits=[f"sg_{g}_{hb}"], event=f"t1_{g}_{hb}")

                t2w = [f"u_{g}_{hb}", f"t1_{g}_{hb}"]
                if hb >= 2:
                    t2w.append(f"dn_{g}_{hb - 2}")

                def t2f(dv, hb=hb, par=par):
                    return dv.tensor_tensor(hbf[:, par * GROUP:(par + 1) * GROUP],
                                            gsb[:, par * GROUP:(par + 1) * GROUP],
                                            pu[par],
                                            ALU.mult)
                p.add("dv", t2f, waits=t2w, event=f"t2_{g}_{hb}")

            emit_down(NHB - 1)
            emit_dummy(AF.Gelu, f"dmy_gelu{g}")

        gl = ngroups - 1
        emit_ln2s(gl, 0)
        emit_ln2s(gl, 1)
        emit_sq2(gl, 0)
        emit_sq2(gl, 1)
        rstd2_ev, m2_ev, m2c, rs2c = ln_stats_pair(
            gl, 1, [f"ln2s{gl}_0", f"ln2s{gl}_1"], [f"sq2_{gl}_0", f"sq2_{gl}_1"])
        emit_ln2_nrm(gl, 0, rstd2_ev, m2_ev, m2c, rs2c,
                     f"out{2 * gl - 2}" if gl > 0 else f"nrm1_{gl}_0")
        emit_ln2_nrm(gl, 1, rstd2_ev, m2_ev, m2c, rs2c,
                     f"out{2 * gl - 1}" if gl > 0 else f"nrm1_{gl}_1")

        p.ev["_all_soa"] = ("soa", p.cnt["soa"])
        p.ev["_all_sob"] = ("sob", p.cnt["sob"])
        p.ops["sy"].append((None, ["_all_soa", "_all_sob"], None, 0))

        with nc.Block() as block:
            @block.tensor
            def _(tensor):
                _run(p, "pe", tensor, sems)

            @block.vector
            def _(vector):
                _run(p, "dv", vector, sems)

            @block.scalar
            def _(scalar):
                _run(p, "sc", scalar, sems)

            @block.sync
            def _(sync):
                _run(p, "sy", sync, sems)

            @block.gpsimd
            def _(gpsimd):
                _run(p, "gp", gpsimd, sems)

    return nc


def _run(p, eng, engobj, sems):
    satisfied = {}
    for fn, waits, sem, n in p.ops[eng]:
        merged = {}
        for name in waits:
            sname, val = p.ev[name]
            if val > merged.get(sname, 0):
                merged[sname] = val
        for sname, val in merged.items():
            if satisfied.get(sname, 0) >= val:
                continue
            engobj.wait_ge(sems[sname], val)
            satisfied[sname] = val
        if fn is None:
            continue
        instr = fn(engobj)
        if sem is not None:
            instr.then_inc(sems[sem], n)


def kernel(L0_attn_x, L0_attn_W_K, L0_attn_W_V, L0_attn_W1, L0_attn_W2,
           L0_attn_m1, L0_attn_m2, L0_ffn_w_gate, L0_ffn_w_up, L0_ffn_w_down):
    x = np.ascontiguousarray(np.asarray(L0_attn_x, dtype=np.float32))
    B, S, _ = x.shape
    rows = S
    wkT = np.ascontiguousarray(np.asarray(L0_attn_W_K, dtype=np.float32).T)
    w1T = np.ascontiguousarray(np.asarray(L0_attn_W1, dtype=np.float32).T)
    w2T = np.ascontiguousarray(np.asarray(L0_attn_W2, dtype=np.float32).T)
    wgT = np.ascontiguousarray(np.asarray(L0_ffn_w_gate, dtype=np.int32).T)
    wuT = np.ascontiguousarray(np.asarray(L0_ffn_w_up, dtype=np.int32).T)
    wdT = np.ascontiguousarray(np.asarray(L0_ffn_w_down, dtype=np.int32).T)
    idf = np.eye(128, dtype=np.float32)

    nc = build_core(rows)
    in_maps = []
    for i in range(B):
        xi = np.ascontiguousarray(x[i])
        in_maps.append({
            "x": xi,
            "xT": np.ascontiguousarray(xi.T),
            "wkT": wkT, "w1T": w1T, "w2T": w2T,
            "wgT": wgT, "wuT": wuT, "wdT": wdT,
            "idf": idf,
        })
    res = run_bass_kernel_spmd(nc, in_maps, core_ids=list(range(B)))
    out = np.stack([res.results[i]["out"] for i in range(B)], axis=0)
    return out.astype(np.float32)


# version 13
# speedup vs baseline: 1.0158x; 1.0158x over previous
import numpy as np
from contextlib import ExitStack

import concourse.bass as bass
import concourse.mybir as mybir
from concourse.bass_utils import run_bass_kernel_spmd

F32 = mybir.dt.float32
BF16 = mybir.dt.bfloat16
I32 = mybir.dt.int32
AF = mybir.ActivationFunctionType
ALU = mybir.AluOpType

D = 1024
HID = 3072
NHB = HID // 128
NDC = D // 128
GROUP = 256
NT = GROUP // 128
EPS = 1e-5
SQ2I = 0.7071067811865476
INV127 = 1.0 / 127.0
INV127_3 = INV127 ** 3


class Prog:

    ENGINES = ("pe", "dv", "sc", "sy", "gp")

    def __init__(self):
        self.ops = {e: [] for e in self.ENGINES}
        self.cnt = {"pe": 0, "dv": 0, "sc": 0,
                    "swk": 0, "sw1": 0, "sw2": 0, "swg": 0, "swu": 0, "swd": 0,
                    "sidf": 0, "sxa": 0, "sxb": 0, "sxta": 0, "sxtb": 0,
                    "soa": 0, "sob": 0}
        self.ev = {}

    def add(self, eng, fn, waits=(), event=None, sem=None, n=1):
        if sem is None:
            sem = {"pe": "pe", "dv": "dv", "sc": "sc"}.get(eng)
        w = [name for name in waits if name is not None]
        self.ops[eng].append((fn, w, sem, n))
        if sem is not None:
            self.cnt[sem] += n
            if event is not None:
                self.ev[event] = (sem, self.cnt[sem])
        else:
            assert event is None


def build_core(rows=4096):
    assert rows % GROUP == 0
    ngroups = rows // GROUP
    nc = bass.Bass("TRN2", target_bir_lowering=False)

    x_d = nc.dram_tensor("x", [rows, D], F32, kind="ExternalInput").ap()
    xT_d = nc.dram_tensor("xT", [D, rows], F32, kind="ExternalInput").ap()
    wk_d = nc.dram_tensor("wk", [D, D], F32, kind="ExternalInput").ap()
    w1T_d = nc.dram_tensor("w1T", [D, 64], F32, kind="ExternalInput").ap()
    w2T_d = nc.dram_tensor("w2T", [64, D], F32, kind="ExternalInput").ap()
    wgT_d = nc.dram_tensor("wgT", [D, HID], I32, kind="ExternalInput").ap()
    wuT_d = nc.dram_tensor("wuT", [D, HID], I32, kind="ExternalInput").ap()
    wdT_d = nc.dram_tensor("wdT", [HID, D], I32, kind="ExternalInput").ap()
    idf_d = nc.dram_tensor("idf", [128, 128], F32, kind="ExternalInput").ap()
    out_d = nc.dram_tensor("out", [rows, D], F32, kind="ExternalOutput").ap()

    p = Prog()

    with ExitStack() as ctx:
        e = ctx.enter_context
        W1T = e(nc.sbuf_tensor("W1T", [128, NDC * 64], BF16))
        W1KT = e(nc.sbuf_tensor("W1KT", [128, NDC * 64], BF16))
        W2T = e(nc.sbuf_tensor("W2T", [128, D], BF16))
        WGT = e(nc.sbuf_tensor("WGT", [128, NDC * HID], BF16))
        WUT = e(nc.sbuf_tensor("WUT", [128, NDC * HID], BF16))
        WDT = e(nc.sbuf_tensor("WDT", [128, NHB * D], BF16))
        IDF = e(nc.sbuf_tensor("IDF", [128, 128], F32))
        xTb = e(nc.sbuf_tensor("xTb", [128, 2 * NDC * GROUP], BF16))
        x1T = e(nc.sbuf_tensor("x1T", [128, NDC * GROUP], BF16))
        r1 = e(nc.sbuf_tensor("r1", [128, 2 * D], F32))
        x1 = e(nc.sbuf_tensor("x1", [128, 4 * D], F32))
        fft = e(nc.sbuf_tensor("fft", [128, 6 * GROUP], BF16))
        aT = fft.ap()[:, 0:GROUP]
        sgb = fft.ap()[:, 0:2 * GROUP]
        gsb = fft.ap()[:, 2 * GROUP:4 * GROUP]
        hbf = fft.ap()[:, 4 * GROUP:6 * GROUP]
        dmp = fft.ap()[:, 0:D]
        st_ = e(nc.sbuf_tensor("st_", [128, 66], F32))
        pA = e(nc.psum_tensor("pA", [128, 1024], F32))
        pB = e(nc.psum_tensor("pB", [128, 1024], F32))
        pa0 = e(nc.psum_tensor("pa0", [128, 1024], F32))
        pa1 = e(nc.psum_tensor("pa1", [128, 1024], F32))
        pa = [pa0, pa1]
        pstA = [pA, pB]
        pkslc = [pA.ap()[:, 0:GROUP], pA.ap()[:, 512:512 + GROUP]]
        pg = [pA.ap()[:, 0:GROUP], pB.ap()[:, 0:GROUP]]
        pu = [pA.ap()[:, 512:512 + GROUP], pB.ap()[:, 512:512 + GROUP]]
        wk_sb = e(nc.sbuf_tensor("wk_sb", [128, NDC * D], BF16))
        sems = {}
        for sname in ("pe", "dv", "sc", "swk", "sw1", "sw2", "swg", "swu",
                      "swd", "sidf", "sxa", "sxb", "sxta", "sxtb", "soa", "sob"):
            sems[sname] = e(nc.semaphore(sname))

        stats_col = {}

        def xsl(T):
            return x1.ap()[:, (T % 4) * D:((T % 4) + 1) * D]

        def stcol(g, t, ln, k):
            return (g % 2) * 32 + ln * 16 + k * 2 + t


        def ln_stats_pair(g, ln, sum_evs, sq_evs, extra=(), after_m=None):
            b0 = (g % 2) * 32 + ln * 16
            tag = f"P{ln}_{g}"
            p.add("dv", lambda dv, b0=b0: dv.tensor_scalar_mul(
                st_[:, b0 + 4:b0 + 6], st_[:, b0 + 0:b0 + 2], 1.0 / D),
                waits=list(sum_evs) + list(extra), event=f"m{tag}")
            if after_m is not None:
                after_m()
            p.add("dv", lambda dv, b0=b0: dv.tensor_scalar(
                st_[:, b0 + 6:b0 + 8], st_[:, b0 + 2:b0 + 4], 1.0 / D, EPS,
                ALU.mult, ALU.add),
                waits=list(sq_evs), event=f"q{tag}")
            p.add("dv", lambda dv, b0=b0: dv.tensor_tensor(
                st_[:, b0 + 8:b0 + 10], st_[:, b0 + 4:b0 + 6], st_[:, b0 + 4:b0 + 6],
                ALU.mult),
                waits=[f"m{tag}"], event=f"mm{tag}")
            p.add("dv", lambda dv, b0=b0: dv.tensor_tensor(
                st_[:, b0 + 10:b0 + 12], st_[:, b0 + 6:b0 + 8], st_[:, b0 + 8:b0 + 10],
                ALU.subtract),
                waits=[f"q{tag}", f"mm{tag}"], event=f"tv{tag}")
            p.add("dv", lambda dv, b0=b0: dv.reciprocal(
                st_[:, b0 + 12:b0 + 14], st_[:, b0 + 10:b0 + 12]),
                waits=[f"tv{tag}"], event=f"rc{tag}")
            p.add("sc", lambda sc, b0=b0: sc.activation(
                st_[:, b0 + 14:b0 + 16], st_[:, b0 + 12:b0 + 14], AF.Sqrt),
                waits=[f"rc{tag}"], event=f"rstd{tag}")
            return f"rstd{tag}", f"m{tag}", b0 + 4, b0 + 14

        def emit_ln2s(gq, t):
            c0 = stcol(gq, t, 1, 0)
            w = [f"dn_{gq}_{NHB - 1}", f"nrm1_{gq}_{t}"]
            sl = xsl(2 * gq + t)

            def r2stt(dv, sl=sl, t=t, c0=c0):
                return dv.scalar_tensor_tensor(
                    sl, pa[t][:, :], INV127_3, sl,
                    ALU.mult, ALU.add,
                    accum_out=st_[:, c0:c0 + 1])
            p.add("dv", r2stt, waits=w, event=f"ln2s{gq}_{t}")

        def emit_sq2(gq, t):
            c0 = stcol(gq, t, 1, 1)

            def sq2(sc, t=t, c0=c0):
                return sc.activation(dmp, xsl(2 * gq + t), AF.Square,
                                     accum_out=st_[:, c0:c0 + 1])
            p.add("sc", sq2, waits=[f"ln2s{gq}_{t}", last_sq[0]],
                  event=f"sq2_{gq}_{t}")
            last_sq[0] = f"sq2_{gq}_{t}"

        def emit_ln2_nrm(gq, t, rstd_ev, m_ev, m_col, rs_col, r1_free_ev):
            T = 2 * gq + t
            dst = r1.ap()[:, t * D:(t + 1) * D]

            def nrm2(dv, t=t, dst=dst, m_col=m_col, rs_col=rs_col):
                return dv.tensor_scalar(
                    dst, xsl(2 * gq + t),
                    st_[:, m_col + t:m_col + t + 1], st_[:, rs_col + t:rs_col + t + 1],
                    ALU.subtract, ALU.mult)
            p.add("dv", nrm2,
                  waits=[rstd_ev, m_ev, f"ln2s{gq}_{t}", r1_free_ev],
                  event=f"nrm2_{gq}_{t}")
            p.add("sy", lambda sy, T=T, dst=dst: sy.dma_start(
                out=out_d[T * 128:(T + 1) * 128, :], in_=dst),
                waits=[f"nrm2_{gq}_{t}"], sem=("soa" if T % 2 == 0 else "sob"),
                n=16, event=f"out{T}")

        def dma_cast(eng_ap_out, in_ap):
            def fn(gp):
                return gp.dma_start(out=eng_ap_out, in_=in_ap)
            return fn

        wk3 = wk_d.rearrange("(c p) h -> c p h", p=128)
        for c in range(NDC):
            p.add("gp", dma_cast(wk_sb.ap()[:, c * D:(c + 1) * D], wk3[c]),
                  sem="swk", n=16, event=f"w_wk{c}")
        w1T3 = w1T_d.rearrange("(c p) m -> c p m", p=128)
        for c in range(NDC):
            p.add("gp", dma_cast(W1T.ap()[:, c * 64:(c + 1) * 64], w1T3[c]),
                  sem="sw1", n=16, event=f"w_w1{c}")
        p.ev["w_w1"] = p.ev[f"w_w1{NDC - 1}"]
        for hc in range(NDC):
            bank = pA.ap()[:, 0:64] if hc % 2 == 0 else pA.ap()[:, 512:576]

            def wkmm(pe, hc=hc, bank=bank):
                ins = None
                for kc in range(NDC):
                    ins = pe.matmul(
                        bank,
                        wk_sb[:, kc * D + hc * 128: kc * D + hc * 128 + 128],
                        W1T[:, kc * 64:(kc + 1) * 64],
                        start=(kc == 0), stop=(kc == NDC - 1))
                return ins
            ww = [f"w_wk{c}" for c in range(NDC)] + ["w_w1"]
            if hc >= 2:
                ww.append(f"wkev{hc - 2}")
            p.add("pe", wkmm, waits=ww, event=f"wk1_{hc}")
            p.add("dv", lambda dv, hc=hc, bank=bank: dv.tensor_copy(
                W1KT.ap()[:, hc * 64:(hc + 1) * 64], bank),
                waits=[f"wk1_{hc}"], event=f"wkev{hc}")
        p.ev["w_fold"] = p.ev[f"wkev{NDC - 1}"]
        p.add("gp", dma_cast(W2T.ap()[0:64, :], w2T_d), sem="sw2", n=16, event="w_w2raw")
        xT3 = xT_d.rearrange("(c p) r -> c p r", p=128)

        def load_xT(g, waits):
            sem = "sxta" if g % 2 == 0 else "sxtb"
            half = (g % 2) * NDC * GROUP
            for c in range(NDC):
                in_ap = xT3[c][:, g * GROUP:(g + 1) * GROUP]
                out_ap = xTb.ap()[:, half + c * GROUP: half + (c + 1) * GROUP]
                p.add("gp", dma_cast(out_ap, in_ap), waits=(waits if c == 0 else ()),
                      sem=sem, n=16, event=f"xT{g}_{c}")

        load_xT(0, [])
        load_xT(1, [])
        wgT3 = wgT_d.rearrange("(c p) h -> c p h", p=128)
        wuT3 = wuT_d.rearrange("(c p) h -> c p h", p=128)
        for c in range(NDC):
            p.add("gp", dma_cast(WGT.ap()[:, c * HID:(c + 1) * HID], wgT3[c]),
                  sem="swg", n=16, event=f"w_wg{c}")
        for c in range(NDC):
            p.add("gp", dma_cast(WUT.ap()[:, c * HID:(c + 1) * HID], wuT3[c]),
                  sem="swu", n=16, event=f"w_wu{c}")
        wdT3 = wdT_d.rearrange("(c p) d -> c p d", p=128)
        for c in range(NHB):
            p.add("gp", dma_cast(WDT.ap()[:, c * D:(c + 1) * D], wdT3[c]),
                  sem="swd", n=16, event=f"w_wd{c}")
        p.ev["w_w2"] = p.ev["w_w2raw"]
        p.add("sy", lambda sy: sy.dma_start(out=IDF[:], in_=idf_d),
              sem="sidf", n=16, event="w_idf")


        def load_x(T):
            buf = r1.ap()[:, (T % 2) * D:((T % 2) + 1) * D]
            if T - 4 >= 0:
                w = [f"out{T - 4}"]
            elif T >= 2:
                w = [f"nrm1_0_{T - 2}", f"sq1_0_{T - 2}"]
            else:
                w = []
            p.add("sy", lambda sy, b=buf, T=T: sy.dma_start(
                out=b, in_=x_d[T * 128:(T + 1) * 128, :]),
                waits=w, sem=("sxa" if T % 2 == 0 else "sxb"), n=16,
                event=f"x{T}")

        load_x(0)
        load_x(1)

        last_sq = [None]
        last_dmy = [None]

        def emit_dummy(func, name):
            def fn(sc):
                return sc.activation(st_[:, 64:65], IDF[:, 0:1], func)
            p.add("sc", fn, waits=[w for w in ("w_idf", last_dmy[0]) if w],
                  event=name)
            last_dmy[0] = name

        for g in range(ngroups):
            h1_waits = [f"xT{g}_{c}" for c in range(NDC)] + ["w_fold"]
            if g > 0:
                h1_waits += [f"t1_{g - 1}_{NHB - 1}", f"sg_{g - 1}_{NHB - 1}"]
            xb = xTb.ap()[:, (g % 2) * NDC * GROUP:(g % 2 + 1) * NDC * GROUP]

            def h1mm(pe, xb=xb):
                ins = None
                for dc in range(NDC):
                    ins = pe.matmul(
                        pB[0:64, 0:GROUP],
                        W1KT[:, dc * 64:(dc + 1) * 64],
                        xb[:, dc * GROUP:(dc + 1) * GROUP],
                        start=(dc == 0), stop=(dc == NDC - 1))
                return ins
            p.add("pe", h1mm, waits=h1_waits, event=f"h1{g}")

            aT_waits = [f"h1{g}"]
            if g > 0:
                aT_waits += [f"ao{g - 1}_1", f"t1_{g - 1}_{NHB - 2}"]
            p.add("sc", lambda sc: sc.activation(aT[0:64, :],
                                                 pB[0:64, 0:GROUP], AF.Gelu),
                  waits=aT_waits, event=f"aT{g}")
            emit_dummy(AF.Sqrt, f"dmy_sqrt{g}")

            for t in range(NT):
                waits = [f"aT{g}", "w_w2"]
                if t == 0 and g > 0:
                    waits += [f"t2_{g - 1}_{NHB - 2}", f"sg_{g - 1}_{NHB - 2}"]
                elif t == 1 and g > 0:
                    waits += [f"t2_{g - 1}_{NHB - 1}", f"sg_{g - 1}_{NHB - 1}"]

                def aomm(pe, t=t):
                    ins = None
                    for sl in range(2):
                        ins = pe.matmul(
                            pstA[t][:, sl * 512:(sl + 1) * 512],
                            aT[0:64, t * 128:(t + 1) * 128],
                            W2T[0:64, sl * 512:(sl + 1) * 512],
                            start=True, stop=True)
                    return ins
                p.add("pe", aomm, waits=waits, event=f"ao{g}_{t}")

            def r1stt_fn(t, T, c0):
                def f(dv):
                    half = r1.ap()[:, (T % 2) * D:((T % 2) + 1) * D]
                    return dv.scalar_tensor_tensor(
                        half, pstA[t][:, :], 1.0, half,
                        ALU.mult, ALU.add,
                        accum_out=st_[:, c0:c0 + 1])
                return f

            T0, T1 = 2 * g, 2 * g + 1
            p.add("dv", r1stt_fn(0, T0, stcol(g, 0, 0, 0)),
                  waits=[f"ao{g}_0", f"x{T0}"], event=f"r1_{g}_0")
            p.add("dv", r1stt_fn(1, T1, stcol(g, 1, 0, 0)),
                  waits=[f"ao{g}_1", f"x{T1}"], event=f"r1_{g}_1")

            def sq1_fn(t, c0):
                def f(sc):
                    return sc.activation(dmp, r1[:, t * D:(t + 1) * D], AF.Square,
                                         accum_out=st_[:, c0:c0 + 1])
                return f
            sq1w = [f"r1_{g}_0", f"ao{g}_1", last_sq[0]]
            if g > 0:
                sq1w.append(f"t2_{g - 1}_{NHB - 1}")
            p.add("sc", sq1_fn(0, stcol(g, 0, 0, 1)), waits=sq1w, event=f"sq1_{g}_0")
            p.add("sc", sq1_fn(1, stcol(g, 1, 0, 1)),
                  waits=[f"r1_{g}_1", f"sq1_{g}_0"], event=f"sq1_{g}_1")
            last_sq[0] = f"sq1_{g}_1"

            rstd1_ev, m1_ev, m1c, rs1c = ln_stats_pair(
                g, 0, [f"r1_{g}_0", f"r1_{g}_1"], [f"sq1_{g}_0", f"sq1_{g}_1"])

            def nrm1_fn(t, T, m_col, rs_col):
                def f(dv):
                    return dv.tensor_scalar(
                        xsl(T), r1[:, t * D:(t + 1) * D],
                        st_[:, m_col + t:m_col + t + 1],
                        st_[:, rs_col + t:rs_col + t + 1],
                        ALU.subtract, ALU.mult)
                return f
            nw0 = [rstd1_ev, m1_ev, f"r1_{g}_0", f"sq1_{g}_0"]
            if g >= 2:
                nw0.append(f"nrm2_{g - 2}_0")
            p.add("dv", nrm1_fn(0, T0, m1c, rs1c), waits=nw0, event=f"nrm1_{g}_0")
            nw1 = [rstd1_ev, f"r1_{g}_1", f"sq1_{g}_1"]
            if g >= 2:
                nw1.append(f"nrm2_{g - 2}_1")
            p.add("dv", nrm1_fn(1, T1, m1c, rs1c), waits=nw1, event=f"nrm1_{g}_1")

            if g > 0:
                emit_ln2s(g - 1, 0)
                emit_ln2s(g - 1, 1)
                emit_sq2(g - 1, 0)
                emit_sq2(g - 1, 1)
                rstd2_ev, m2_ev, m2c, rs2c = ln_stats_pair(
                    g - 1, 1, [f"ln2s{g - 1}_0", f"ln2s{g - 1}_1"],
                    [f"sq2_{g - 1}_0", f"sq2_{g - 1}_1"])
                emit_ln2_nrm(g - 1, 0, rstd2_ev, m2_ev, m2c, rs2c, f"nrm1_{g}_0")
                emit_ln2_nrm(g - 1, 1, rstd2_ev, m2_ev, m2c, rs2c, f"nrm1_{g}_1")

            emit_dummy(AF.Sigmoid, f"dmy_sig{g}")

            for t in range(NT):
                for rd in range(2):
                    tw = [f"nrm1_{g}_{t}", "w_idf"]
                    if t == 0 and rd == 0:
                        tw.append(f"r1_{g}_1")
                    else:
                        tw.append(f"xev{g}_{t}_{rd - 1}" if rd == 1 else f"xev{g}_{t - 1}_1")

                    def xtp(pe, t=t, rd=rd, T=2 * g + t):
                        base = (T % 4) * D
                        ins = None
                        for q in range(4):
                            dc = rd * 4 + q
                            ins = pe.transpose(
                                pB[:, 512 + q * 128: 512 + (q + 1) * 128],
                                x1[:, base + dc * 128: base + dc * 128 + 128],
                                IDF[:])
                        return ins
                    p.add("pe", xtp, waits=tw, event=f"xtp{g}_{t}_{rd}")

                    def xev(eng, t=t, rd=rd):
                        src = pB.ap()[:, 512:1024].rearrange("p (q j) -> p q j", j=128)
                        dst = x1T.ap().rearrange("p (c r) -> p c r", r=GROUP)[
                            :, rd * 4:(rd + 1) * 4, t * 128:(t + 1) * 128]
                        if rd == 0:
                            return eng.copy(dst, src)
                        return eng.tensor_copy(dst, src)
                    p.add("sc" if rd == 0 else "dv", xev, waits=[f"xtp{g}_{t}_{rd}"],
                          event=f"xev{g}_{t}_{rd}")

            x1T_ready = [f"xev{g}_{t}_{rd}" for t in range(NT) for rd in range(2)]

            if g + 1 < ngroups:
                load_x(2 * (g + 1))
                load_x(2 * (g + 1) + 1)
            if g + 2 < ngroups:
                load_xT(g + 2, [f"h1{g}"])

            wg_waits = [f"w_wg{c}" for c in range(NDC)]
            wu_waits = [f"w_wu{c}" for c in range(NDC)]

            def emit_down(hb):
                par = hb % 2
                dw = [f"t2_{g}_{hb}", f"w_wd{NHB - 1}"]
                if hb == 0 and g > 0:
                    dw.append(f"ln2s{g - 1}_{NT - 1}")

                def dnf(pe, hb=hb, par=par):
                    ins = None
                    for t in range(NT):
                        for sl in range(2):
                            ins = pe.matmul(
                                pa[t][:, sl * 512:(sl + 1) * 512],
                                hbf[:, par * GROUP + t * 128: par * GROUP + t * 128 + 128],
                                WDT[:, hb * D + sl * 512: hb * D + sl * 512 + 512],
                                start=(hb == 0), stop=(hb == NHB - 1))
                    return ins
                p.add("pe", dnf, waits=dw, event=f"dn_{g}_{hb}")

            for hb in range(NHB):
                par = hb % 2
                gw = list(x1T_ready)
                if g == 0 and hb < 2:
                    gw += wg_waits
                if hb >= 2:
                    gw += [f"sg_{g}_{hb - 2}", f"t1_{g}_{hb - 2}"]

                def gmm(pe, hb=hb, par=par):
                    ins = None
                    for dc in range(NDC):
                        ins = pe.matmul(
                            pg[par],
                            WGT[:, dc * HID + hb * 128: dc * HID + hb * 128 + 128],
                            x1T[:, dc * GROUP:(dc + 1) * GROUP],
                            start=(dc == 0), stop=(dc == NDC - 1))
                    return ins
                p.add("pe", gmm, waits=gw, event=f"g_{g}_{hb}")

                uw = []
                if g == 0 and hb < 2:
                    uw += wu_waits
                if hb >= 2:
                    uw.append(f"t2_{g}_{hb - 2}")

                def umm(pe, hb=hb, par=par):
                    ins = None
                    for dc in range(NDC):
                        ins = pe.matmul(
                            pu[par],
                            WUT[:, dc * HID + hb * 128: dc * HID + hb * 128 + 128],
                            x1T[:, dc * GROUP:(dc + 1) * GROUP],
                            start=(dc == 0), stop=(dc == NDC - 1))
                    return ins
                p.add("pe", umm, waits=uw, event=f"u_{g}_{hb}")

                if hb >= 1:
                    emit_down(hb - 1)

                sgw = [f"g_{g}_{hb}"]
                if hb >= 2:
                    sgw.append(f"t1_{g}_{hb - 2}")
                elif hb == 0:
                    sgw.append(last_sq[0])

                def sgf(sc, hb=hb, par=par):
                    return sc.activation(sgb[:, par * GROUP:(par + 1) * GROUP],
                                         pg[par], AF.Sigmoid, scale=INV127)
                p.add("sc", sgf, waits=sgw, event=f"sg_{g}_{hb}")

                def t1f(dv, hb=hb, par=par):
                    return dv.tensor_tensor(gsb[:, par * GROUP:(par + 1) * GROUP],
                                            pg[par],
                                            sgb[:, par * GROUP:(par + 1) * GROUP],
                                            ALU.mult)
                p.add("dv", t1f, waits=[f"sg_{g}_{hb}"], event=f"t1_{g}_{hb}")

                t2w = [f"u_{g}_{hb}", f"t1_{g}_{hb}"]
                if hb >= 2:
                    t2w.append(f"dn_{g}_{hb - 2}")

                def t2f(dv, hb=hb, par=par):
                    return dv.tensor_tensor(hbf[:, par * GROUP:(par + 1) * GROUP],
                                            gsb[:, par * GROUP:(par + 1) * GROUP],
                                            pu[par],
                                            ALU.mult)
                p.add("dv", t2f, waits=t2w, event=f"t2_{g}_{hb}")

            emit_down(NHB - 1)
            emit_dummy(AF.Gelu, f"dmy_gelu{g}")

        gl = ngroups - 1
        emit_ln2s(gl, 0)
        emit_ln2s(gl, 1)
        emit_sq2(gl, 0)
        emit_sq2(gl, 1)
        rstd2_ev, m2_ev, m2c, rs2c = ln_stats_pair(
            gl, 1, [f"ln2s{gl}_0", f"ln2s{gl}_1"], [f"sq2_{gl}_0", f"sq2_{gl}_1"])
        emit_ln2_nrm(gl, 0, rstd2_ev, m2_ev, m2c, rs2c,
                     f"out{2 * gl - 2}" if gl > 0 else f"nrm1_{gl}_0")
        emit_ln2_nrm(gl, 1, rstd2_ev, m2_ev, m2c, rs2c,
                     f"out{2 * gl - 1}" if gl > 0 else f"nrm1_{gl}_1")

        p.ev["_all_soa"] = ("soa", p.cnt["soa"])
        p.ev["_all_sob"] = ("sob", p.cnt["sob"])
        p.ops["sy"].append((None, ["_all_soa", "_all_sob"], None, 0))

        with nc.Block() as block:
            @block.tensor
            def _(tensor):
                _run(p, "pe", tensor, sems)

            @block.vector
            def _(vector):
                _run(p, "dv", vector, sems)

            @block.scalar
            def _(scalar):
                _run(p, "sc", scalar, sems)

            @block.sync
            def _(sync):
                _run(p, "sy", sync, sems)

            @block.gpsimd
            def _(gpsimd):
                _run(p, "gp", gpsimd, sems)

    return nc


def _run(p, eng, engobj, sems):
    satisfied = {}
    for fn, waits, sem, n in p.ops[eng]:
        merged = {}
        for name in waits:
            sname, val = p.ev[name]
            if val > merged.get(sname, 0):
                merged[sname] = val
        for sname, val in merged.items():
            if satisfied.get(sname, 0) >= val:
                continue
            engobj.wait_ge(sems[sname], val)
            satisfied[sname] = val
        if fn is None:
            continue
        instr = fn(engobj)
        if sem is not None:
            instr.then_inc(sems[sem], n)


def kernel(L0_attn_x, L0_attn_W_K, L0_attn_W_V, L0_attn_W1, L0_attn_W2,
           L0_attn_m1, L0_attn_m2, L0_ffn_w_gate, L0_ffn_w_up, L0_ffn_w_down):
    x = np.ascontiguousarray(np.asarray(L0_attn_x, dtype=np.float32))
    B, S, _ = x.shape
    rows = S
    wk = np.ascontiguousarray(np.asarray(L0_attn_W_K, dtype=np.float32))
    w1T = np.ascontiguousarray(np.asarray(L0_attn_W1, dtype=np.float32).T)
    w2T = np.ascontiguousarray(np.asarray(L0_attn_W2, dtype=np.float32).T)
    wgT = np.ascontiguousarray(np.asarray(L0_ffn_w_gate, dtype=np.int32).T)
    wuT = np.ascontiguousarray(np.asarray(L0_ffn_w_up, dtype=np.int32).T)
    wdT = np.ascontiguousarray(np.asarray(L0_ffn_w_down, dtype=np.int32).T)
    idf = np.eye(128, dtype=np.float32)

    nc = build_core(rows)
    in_maps = []
    for i in range(B):
        xi = np.ascontiguousarray(x[i])
        in_maps.append({
            "x": xi,
            "xT": np.ascontiguousarray(xi.T),
            "wk": wk, "w1T": w1T, "w2T": w2T,
            "wgT": wgT, "wuT": wuT, "wdT": wdT,
            "idf": idf,
        })
    res = run_bass_kernel_spmd(nc, in_maps, core_ids=list(range(B)))
    out = np.stack([res.results[i]["out"] for i in range(B)], axis=0)
    return out.astype(np.float32)


# version 14 (final)
# speedup vs baseline: 1.0876x; 1.0707x over previous
import numpy as np
from contextlib import ExitStack

import concourse.bass as bass
import concourse.mybir as mybir
from concourse.bass_utils import run_bass_kernel_spmd

F32 = mybir.dt.float32
BF16 = mybir.dt.bfloat16
I32 = mybir.dt.int32
AF = mybir.ActivationFunctionType
ALU = mybir.AluOpType

D = 1024
HID = 3072
NHB = HID // 128
NDC = D // 128
GROUP = 256
NT = GROUP // 128
EPS = 1e-5
INV127 = 1.0 / 127.0
INV127_3 = INV127 ** 3


class Prog:

    ENGINES = ("pe", "dv", "sc", "sy", "gp")

    def __init__(self):
        self.ops = {e: [] for e in self.ENGINES}
        self.cnt = {"pe": 0, "dv": 0, "sc": 0,
                    "swk": 0, "sw1": 0, "sw2": 0, "swg": 0, "swu": 0, "swd": 0,
                    "sidf": 0, "sxa": 0, "sxb": 0, "sxta": 0, "sxtb": 0,
                    "soa": 0, "sob": 0}
        self.ev = {}

    def add(self, eng, fn, waits=(), event=None, sem=None, n=1):
        if sem is None:
            sem = {"pe": "pe", "dv": "dv", "sc": "sc"}.get(eng)
        w = [name for name in waits if name is not None]
        self.ops[eng].append((fn, w, sem, n))
        if sem is not None:
            self.cnt[sem] += n
            if event is not None:
                self.ev[event] = (sem, self.cnt[sem])
        else:
            assert event is None


def build_core(rows=4096):
    assert rows % GROUP == 0
    ngroups = rows // GROUP
    assert ngroups >= 3
    nc = bass.Bass("TRN2", target_bir_lowering=False)

    x_d = nc.dram_tensor("x", [rows, D], F32, kind="ExternalInput").ap()
    xT_d = nc.dram_tensor("xT", [D, rows], F32, kind="ExternalInput").ap()
    wk_d = nc.dram_tensor("wk", [D, D], F32, kind="ExternalInput").ap()
    w1T_d = nc.dram_tensor("w1T", [D, 64], F32, kind="ExternalInput").ap()
    w2T_d = nc.dram_tensor("w2T", [64, D], F32, kind="ExternalInput").ap()
    wgT_d = nc.dram_tensor("wgT", [D, HID], I32, kind="ExternalInput").ap()
    wuT_d = nc.dram_tensor("wuT", [D, HID], I32, kind="ExternalInput").ap()
    wdT_d = nc.dram_tensor("wdT", [HID, D], I32, kind="ExternalInput").ap()
    idf_d = nc.dram_tensor("idf", [128, 128], F32, kind="ExternalInput").ap()
    out_d = nc.dram_tensor("out", [rows, D], F32, kind="ExternalOutput").ap()

    p = Prog()

    with ExitStack() as ctx:
        e = ctx.enter_context
        W1T = e(nc.sbuf_tensor("W1T", [128, NDC * 64], BF16))
        W1KT = e(nc.sbuf_tensor("W1KT", [128, NDC * 64], BF16))
        W2T = e(nc.sbuf_tensor("W2T", [128, D], BF16))
        WGT = e(nc.sbuf_tensor("WGT", [128, NDC * HID], BF16))
        WUT = e(nc.sbuf_tensor("WUT", [128, NDC * HID], BF16))
        WDT = e(nc.sbuf_tensor("WDT", [128, NHB * D], BF16))
        IDF = e(nc.sbuf_tensor("IDF", [128, 128], F32))
        xTb = e(nc.sbuf_tensor("xTb", [128, 2 * NDC * GROUP], BF16))
        x1T = e(nc.sbuf_tensor("x1T", [128, NDC * GROUP], BF16))
        r1 = e(nc.sbuf_tensor("r1", [128, 2 * D], F32))
        x1 = e(nc.sbuf_tensor("x1", [128, 4 * D], F32))
        aT = e(nc.sbuf_tensor("aT", [128, GROUP], BF16))
        fft = e(nc.sbuf_tensor("fft", [128, 6 * GROUP], BF16))
        sgb = fft.ap()[:, 0:2 * GROUP]
        gsb = fft.ap()[:, 2 * GROUP:4 * GROUP]
        hbf = fft.ap()[:, 4 * GROUP:6 * GROUP]
        dsq = e(nc.sbuf_tensor("dsq", [128, D], BF16))
        st_ = e(nc.sbuf_tensor("st_", [128, 66], F32))
        wk_sb = e(nc.sbuf_tensor("wk_sb", [128, NDC * D], BF16))
        pA = e(nc.psum_tensor("pA", [128, 1024], F32))
        pB = e(nc.psum_tensor("pB", [128, 1024], F32))
        pa0 = e(nc.psum_tensor("pa0", [128, 1024], F32))
        pa1 = e(nc.psum_tensor("pa1", [128, 1024], F32))
        pa = [pa0, pa1]
        pg = pA.ap()[:, 0:GROUP]
        pu = pA.ap()[:, 512:512 + GROUP]
        sems = {}
        for sname in ("pe", "dv", "sc", "swk", "sw1", "sw2", "swg", "swu",
                      "swd", "sidf", "sxa", "sxb", "sxta", "sxtb", "soa", "sob"):
            sems[sname] = e(nc.semaphore(sname))

        def xsl(T):
            return x1.ap()[:, (T % 4) * D:((T % 4) + 1) * D]

        def stcol(g, t, ln, k):
            return (g % 2) * 32 + ln * 16 + k * 2 + t

        last_sq = [None]
        last_dmy = [None]

        def emit_dummy(name):
            def fn(sc):
                return sc.activation(st_[:, 64:65], IDF[:, 0:1], AF.Sigmoid)
            p.add("sc", fn, waits=[w for w in ("w_idf", last_dmy[0]) if w],
                  event=name)
            last_dmy[0] = name

        def ln_stats_pair(g, ln, sum_evs, sq_evs):
            b0 = (g % 2) * 32 + ln * 16
            tag = f"P{ln}_{g}"
            p.add("dv", lambda dv, b0=b0: dv.tensor_scalar_mul(
                st_[:, b0 + 4:b0 + 6], st_[:, b0 + 0:b0 + 2], 1.0 / D),
                waits=list(sum_evs), event=f"m{tag}")
            p.add("dv", lambda dv, b0=b0: dv.tensor_scalar(
                st_[:, b0 + 6:b0 + 8], st_[:, b0 + 2:b0 + 4], 1.0 / D, EPS,
                ALU.mult, ALU.add),
                waits=list(sq_evs), event=f"q{tag}")
            p.add("dv", lambda dv, b0=b0: dv.tensor_tensor(
                st_[:, b0 + 8:b0 + 10], st_[:, b0 + 4:b0 + 6],
                st_[:, b0 + 4:b0 + 6], ALU.mult),
                waits=[f"m{tag}"], event=f"mm{tag}")
            p.add("dv", lambda dv, b0=b0: dv.tensor_tensor(
                st_[:, b0 + 10:b0 + 12], st_[:, b0 + 6:b0 + 8],
                st_[:, b0 + 8:b0 + 10], ALU.subtract),
                waits=[f"q{tag}", f"mm{tag}"], event=f"tv{tag}")
            seed = f"rstdP{ln}_{g - 1}" if g >= 1 else None
            if seed is None or seed not in p.ev:
                p.add("dv", lambda dv, b0=b0: dv.reciprocal(
                    st_[:, b0 + 12:b0 + 14], st_[:, b0 + 10:b0 + 12]),
                    waits=[f"tv{tag}"], event=f"rc{tag}")
                p.add("sc", lambda sc, b0=b0: sc.activation(
                    st_[:, b0 + 14:b0 + 16], st_[:, b0 + 12:b0 + 14], AF.Sqrt),
                    waits=[f"rc{tag}"], event=f"rstd{tag}")
                return f"rstd{tag}", f"m{tag}", b0 + 4, b0 + 14
            s0 = ((g - 1) % 2) * 32 + ln * 16
            p.add("dv", lambda dv, b0=b0, s0=s0: dv.tensor_copy(
                st_[:, b0 + 14:b0 + 16], st_[:, s0 + 14:s0 + 16]),
                waits=[f"tv{tag}", seed], event=f"{tag}y0")
            prev = f"{tag}y0"
            for it in range(3):
                p.add("dv", lambda dv, b0=b0: dv.tensor_tensor(
                    st_[:, b0 + 12:b0 + 14], st_[:, b0 + 14:b0 + 16],
                    st_[:, b0 + 14:b0 + 16], ALU.mult),
                    waits=[prev], event=f"{tag}a{it}")
                p.add("dv", lambda dv, b0=b0: dv.tensor_tensor(
                    st_[:, b0 + 12:b0 + 14], st_[:, b0 + 10:b0 + 12],
                    st_[:, b0 + 12:b0 + 14], ALU.mult),
                    waits=[f"{tag}a{it}"], event=f"{tag}b{it}")
                p.add("dv", lambda dv, b0=b0: dv.tensor_scalar(
                    st_[:, b0 + 12:b0 + 14], st_[:, b0 + 12:b0 + 14], -0.5, 1.5,
                    ALU.mult, ALU.add),
                    waits=[f"{tag}b{it}"], event=f"{tag}c{it}")
                last = f"rstd{tag}" if it == 2 else f"{tag}y{it + 1}"
                p.add("dv", lambda dv, b0=b0: dv.tensor_tensor(
                    st_[:, b0 + 14:b0 + 16], st_[:, b0 + 14:b0 + 16],
                    st_[:, b0 + 12:b0 + 14], ALU.mult),
                    waits=[f"{tag}c{it}"], event=last)
                prev = last
            return f"rstd{tag}", f"m{tag}", b0 + 4, b0 + 14

        def dma_cast(out_ap, in_ap):
            def fn(gp):
                return gp.dma_start(out=out_ap, in_=in_ap)
            return fn

        p.add("sy", lambda sy: sy.dma_start(out=IDF[:], in_=idf_d),
              sem="sidf", n=16, event="w_idf")

        def load_x(T):
            buf = r1.ap()[:, (T % 2) * D:((T % 2) + 1) * D]
            if T - 4 >= 0:
                w = [f"out{T - 4}"]
            elif T >= 2:
                w = [f"nrm1_0_{T - 2}", f"sq1_0_{T - 2}"]
            else:
                w = []
            p.add("sy", lambda sy, b=buf, T=T: sy.dma_start(
                out=b, in_=x_d[T * 128:(T + 1) * 128, :]),
                waits=w, sem=("sxa" if T % 2 == 0 else "sxb"), n=16,
                event=f"x{T}")

        load_x(0)
        load_x(1)

        wk3 = wk_d.rearrange("(c p) h -> c p h", p=128)
        for c in range(NDC):
            p.add("gp", dma_cast(wk_sb.ap()[:, c * D:(c + 1) * D], wk3[c]),
                  sem="swk", n=16, event=f"w_wk{c}")
        w1T3 = w1T_d.rearrange("(c p) m -> c p m", p=128)
        for c in range(NDC):
            p.add("gp", dma_cast(W1T.ap()[:, c * 64:(c + 1) * 64], w1T3[c]),
                  sem="sw1", n=16, event=f"w_w1{c}")
        p.ev["w_w1"] = p.ev[f"w_w1{NDC - 1}"]

        xT3 = xT_d.rearrange("(c p) r -> c p r", p=128)

        def load_xT(g, waits):
            sem = "sxta" if g % 2 == 0 else "sxtb"
            half = (g % 2) * NDC * GROUP
            for c in range(NDC):
                in_ap = xT3[c][:, g * GROUP:(g + 1) * GROUP]
                out_ap = xTb.ap()[:, half + c * GROUP: half + (c + 1) * GROUP]
                p.add("gp", dma_cast(out_ap, in_ap),
                      waits=(waits if c == 0 else ()),
                      sem=sem, n=16, event=f"xT{g}_{c}")

        load_xT(0, [])
        load_xT(1, [])
        p.add("gp", dma_cast(W2T.ap()[0:64, :], w2T_d), sem="sw2", n=16,
              event="w_w2")
        wgT3 = wgT_d.rearrange("(c p) h -> c p h", p=128)
        wuT3 = wuT_d.rearrange("(c p) h -> c p h", p=128)
        for c in range(NDC):
            p.add("gp", dma_cast(WGT.ap()[:, c * HID:(c + 1) * HID], wgT3[c]),
                  sem="swg", n=16, event=f"w_wg{c}")
        for c in range(NDC):
            p.add("gp", dma_cast(WUT.ap()[:, c * HID:(c + 1) * HID], wuT3[c]),
                  sem="swu", n=16, event=f"w_wu{c}")
        wdT3 = wdT_d.rearrange("(c p) d -> c p d", p=128)
        for c in range(NHB):
            p.add("gp", dma_cast(WDT.ap()[:, c * D:(c + 1) * D], wdT3[c]),
                  sem="swd", n=16, event=f"w_wd{c}")

        for hc in range(NDC):
            bank = pA.ap()[:, 0:64] if hc % 2 == 0 else pA.ap()[:, 512:576]

            def wkmm(pe, hc=hc, bank=bank):
                ins = None
                for kc in range(NDC):
                    ins = pe.matmul(
                        bank,
                        wk_sb[:, kc * D + hc * 128: kc * D + hc * 128 + 128],
                        W1T[:, kc * 64:(kc + 1) * 64],
                        start=(kc == 0), stop=(kc == NDC - 1))
                return ins
            ww = [f"w_wk{c}" for c in range(NDC)] + ["w_w1"]
            if hc >= 2:
                ww.append(f"wkev{hc - 2}")
            p.add("pe", wkmm, waits=ww, event=f"wk1_{hc}")
            p.add("dv", lambda dv, hc=hc, bank=bank: dv.tensor_copy(
                W1KT.ap()[:, hc * 64:(hc + 1) * 64], bank),
                waits=[f"wk1_{hc}"], event=f"wkev{hc}")
        p.ev["w_fold"] = p.ev[f"wkev{NDC - 1}"]

        def emit_H1(g, extra=()):
            h1w = [f"xT{g}_{c}" for c in range(NDC)] + ["w_fold"] + list(extra)
            xb = xTb.ap()[:, (g % 2) * NDC * GROUP:(g % 2 + 1) * NDC * GROUP]

            def h1mm(pe, xb=xb):
                ins = None
                for dc in range(NDC):
                    ins = pe.matmul(
                        pB[0:64, 0:GROUP],
                        W1KT[:, dc * 64:(dc + 1) * 64],
                        xb[:, dc * GROUP:(dc + 1) * GROUP],
                        start=(dc == 0), stop=(dc == NDC - 1))
                return ins
            p.add("pe", h1mm, waits=h1w, event=f"h1{g}")

        def emit_gelu(g):
            gw = [f"h1{g}"]
            if g > 0:
                gw.append(f"ao{g - 1}_1")
            p.add("sc", lambda sc: sc.activation(aT[0:64, :],
                                                 pB[0:64, 0:GROUP], AF.Gelu),
                  waits=gw, event=f"aT{g}")

        def emit_AO(g, t, extra=()):
            waits = [f"aT{g}", "w_w2"] + list(extra)

            def aomm(pe, t=t):
                ins = None
                for sl in range(2):
                    ins = pe.matmul(
                        pB[:, sl * 512:(sl + 1) * 512],
                        aT[0:64, t * 128:(t + 1) * 128],
                        W2T[0:64, sl * 512:(sl + 1) * 512],
                        start=True, stop=True)
                return ins
            p.add("pe", aomm, waits=waits, event=f"ao{g}_{t}")

        def emit_r1stt(g, t):
            T = 2 * g + t
            c0 = stcol(g, t, 0, 0)

            def f(dv, T=T, c0=c0):
                half = r1.ap()[:, (T % 2) * D:((T % 2) + 1) * D]
                return dv.scalar_tensor_tensor(
                    half, pB[:, :], 1.0, half,
                    ALU.mult, ALU.add,
                    accum_out=st_[:, c0:c0 + 1])
            p.add("dv", f, waits=[f"ao{g}_{t}", f"x{T}"], event=f"r1_{g}_{t}")

        def emit_sq1(g, t):
            c0 = stcol(g, t, 0, 1)

            def f(sc, t=t, c0=c0):
                return sc.activation(dsq[:], r1[:, t * D:(t + 1) * D],
                                     AF.Square, accum_out=st_[:, c0:c0 + 1])
            p.add("sc", f, waits=[f"r1_{g}_{t}", last_sq[0]],
                  event=f"sq1_{g}_{t}")
            last_sq[0] = f"sq1_{g}_{t}"

        def emit_nrm1(g, t, m_col, rs_col, rstd_ev, m_ev):
            T = 2 * g + t
            nw = [rstd_ev, m_ev, f"r1_{g}_{t}", f"sq1_{g}_{t}"]
            if g >= 2:
                nw.append(f"nrm2_{g - 2}_{t}")

            def f(dv, t=t, T=T, m_col=m_col, rs_col=rs_col):
                return dv.tensor_scalar(
                    xsl(T), r1[:, t * D:(t + 1) * D],
                    st_[:, m_col + t:m_col + t + 1],
                    st_[:, rs_col + t:rs_col + t + 1],
                    ALU.subtract, ALU.mult)
            p.add("dv", f, waits=nw, event=f"nrm1_{g}_{t}")

        def emit_xtp_xev(g):
            for t in range(NT):
                for rd in range(2):
                    off = 512 if rd == 0 else 0
                    tw = [f"nrm1_{g}_{t}", "w_idf"]
                    if t == 0:
                        tw.append(f"r1_{g}_1")
                    else:
                        tw.append(f"xev{g}_0_{rd}")

                    def xtp(pe, t=t, rd=rd, off=off, T=2 * g + t):
                        base = (T % 4) * D
                        ins = None
                        for q in range(4):
                            dc = rd * 4 + q
                            ins = pe.transpose(
                                pB[:, off + q * 128: off + (q + 1) * 128],
                                x1[:, base + dc * 128: base + dc * 128 + 128],
                                IDF[:])
                        return ins
                    p.add("pe", xtp, waits=tw, event=f"xtp{g}_{t}_{rd}")

                    def xev(eng, t=t, rd=rd, off=off):
                        src = pB.ap()[:, off:off + 512].rearrange(
                            "p (q j) -> p q j", j=128)
                        dst = x1T.ap().rearrange("p (c r) -> p c r", r=GROUP)[
                            :, rd * 4:(rd + 1) * 4, t * 128:(t + 1) * 128]
                        if rd == 0:
                            return eng.copy(dst, src)
                        return eng.tensor_copy(dst, src)
                    p.add("sc" if rd == 0 else "dv", xev,
                          waits=[f"xtp{g}_{t}_{rd}"], event=f"xev{g}_{t}_{rd}")

        def emit_ln2s(gq, t):
            c0 = stcol(gq, t, 1, 0)
            sl = xsl(2 * gq + t)

            def f(dv, sl=sl, t=t, c0=c0):
                return dv.scalar_tensor_tensor(
                    sl, pa[t][:, :], INV127_3, sl,
                    ALU.mult, ALU.add,
                    accum_out=st_[:, c0:c0 + 1])
            p.add("dv", f, waits=[f"dn_{gq}_{NHB - 1}", f"nrm1_{gq}_{t}"],
                  event=f"ln2s{gq}_{t}")

        def emit_sq2(gq, t):
            c0 = stcol(gq, t, 1, 1)

            def f(sc, gq=gq, t=t, c0=c0):
                return sc.activation(dsq[:], xsl(2 * gq + t), AF.Square,
                                     accum_out=st_[:, c0:c0 + 1])
            p.add("sc", f, waits=[f"ln2s{gq}_{t}", last_sq[0]],
                  event=f"sq2_{gq}_{t}")
            last_sq[0] = f"sq2_{gq}_{t}"

        def emit_nrm2_out(gq, t, rstd_ev, m_ev, m_col, rs_col, r1_free_ev):
            T = 2 * gq + t
            dst = r1.ap()[:, t * D:(t + 1) * D]

            def f(dv, gq=gq, t=t, dst=dst, m_col=m_col, rs_col=rs_col):
                return dv.tensor_scalar(
                    dst, xsl(2 * gq + t),
                    st_[:, m_col + t:m_col + t + 1],
                    st_[:, rs_col + t:rs_col + t + 1],
                    ALU.subtract, ALU.mult)
            p.add("dv", f, waits=[rstd_ev, m_ev, f"ln2s{gq}_{t}",
                                  f"sq2_{gq}_{t}", r1_free_ev],
                  event=f"nrm2_{gq}_{t}")
            p.add("sy", lambda sy, T=T, dst=dst: sy.dma_start(
                out=out_d[T * 128:(T + 1) * 128, :], in_=dst),
                waits=[f"nrm2_{gq}_{t}"], sem=("soa" if T % 2 == 0 else "sob"),
                n=16, event=f"out{T}")

        emit_H1(0)
        emit_gelu(0)
        emit_AO(0, 0)
        emit_r1stt(0, 0)
        emit_sq1(0, 0)
        emit_AO(0, 1, extra=["r1_0_0"])
        emit_r1stt(0, 1)
        emit_sq1(0, 1)
        emit_H1(1, extra=["r1_0_1"])
        emit_gelu(1)
        st1 = ln_stats_pair(0, 0, ["r1_0_0", "r1_0_1"], ["sq1_0_0", "sq1_0_1"])
        emit_dummy("dmy_sig0")
        emit_nrm1(0, 0, st1[2], st1[3], st1[0], st1[1])
        emit_nrm1(0, 1, st1[2], st1[3], st1[0], st1[1])
        emit_xtp_xev(0)

        wg_waits = [f"w_wg{c}" for c in range(NDC)]
        wu_waits = [f"w_wu{c}" for c in range(NDC)]
        for g in range(ngroups):
            x1T_ready = [f"xev{g}_{t}_{rd}"
                         for t in range(NT) for rd in range(2)]

            def emit_down(hb):
                par = hb % 2
                dw = [f"t2_{g}_{hb}", f"w_wd{NHB - 1}"]
                if hb == 0 and g > 0:
                    dw.append(f"ln2s{g - 1}_{NT - 1}")

                def dnf(pe, hb=hb, par=par):
                    ins = None
                    for t in range(NT):
                        for sl in range(2):
                            ins = pe.matmul(
                                pa[t][:, sl * 512:(sl + 1) * 512],
                                hbf[:, par * GROUP + t * 128:
                                    par * GROUP + t * 128 + 128],
                                WDT[:, hb * D + sl * 512:
                                    hb * D + sl * 512 + 512],
                                start=(hb == 0), stop=(hb == NHB - 1))
                    return ins
                p.add("pe", dnf, waits=dw, event=f"dn_{g}_{hb}")

            have_next = g + 1 < ngroups
            have_prev = g > 0
            for hb in range(NHB):
                par = hb % 2
                gw = list(x1T_ready) if hb == 0 else [f"sg_{g}_{hb - 1}",
                                                      f"t1_{g}_{hb - 1}"]
                if hb == 0:
                    if g == 0:
                        gw += wg_waits
                    else:
                        gw += [f"sg_{g - 1}_{NHB - 1}", f"t1_{g - 1}_{NHB - 1}"]

                def gmm(pe, hb=hb):
                    ins = None
                    for dc in range(NDC):
                        ins = pe.matmul(
                            pg,
                            WGT[:, dc * HID + hb * 128:
                                dc * HID + hb * 128 + 128],
                            x1T[:, dc * GROUP:(dc + 1) * GROUP],
                            start=(dc == 0), stop=(dc == NDC - 1))
                    return ins
                p.add("pe", gmm, waits=gw, event=f"g_{g}_{hb}")

                uw = [f"t2_{g}_{hb - 1}"] if hb >= 1 else (
                    wu_waits if g == 0 else [f"t2_{g - 1}_{NHB - 1}"])

                def umm(pe, hb=hb):
                    ins = None
                    for dc in range(NDC):
                        ins = pe.matmul(
                            pu,
                            WUT[:, dc * HID + hb * 128:
                                dc * HID + hb * 128 + 128],
                            x1T[:, dc * GROUP:(dc + 1) * GROUP],
                            start=(dc == 0), stop=(dc == NDC - 1))
                    return ins
                p.add("pe", umm, waits=uw, event=f"u_{g}_{hb}")

                if hb >= 1:
                    emit_down(hb - 1)

                sgw = [f"g_{g}_{hb}"]
                if hb >= 2:
                    sgw.append(f"t1_{g}_{hb - 2}")
                elif g > 0:
                    sgw.append(f"t1_{g - 1}_{NHB - 2 + hb}")

                def sgf(sc, hb=hb, par=par):
                    return sc.activation(sgb[:, par * GROUP:(par + 1) * GROUP],
                                         pg, AF.Sigmoid, scale=INV127)
                p.add("sc", sgf, waits=sgw, event=f"sg_{g}_{hb}")

                def t1f(dv, hb=hb, par=par):
                    return dv.tensor_tensor(
                        gsb[:, par * GROUP:(par + 1) * GROUP], pg,
                        sgb[:, par * GROUP:(par + 1) * GROUP], ALU.mult)
                p.add("dv", t1f, waits=[f"sg_{g}_{hb}"], event=f"t1_{g}_{hb}")

                t2w = [f"u_{g}_{hb}", f"t1_{g}_{hb}"]
                if hb >= 2:
                    t2w.append(f"dn_{g}_{hb - 2}")
                elif g > 0:
                    t2w.append(f"dn_{g - 1}_{NHB - 2 + hb}")

                def t2f(dv, hb=hb, par=par):
                    return dv.tensor_tensor(
                        hbf[:, par * GROUP:(par + 1) * GROUP],
                        gsb[:, par * GROUP:(par + 1) * GROUP], pu, ALU.mult)
                p.add("dv", t2f, waits=t2w, event=f"t2_{g}_{hb}")

                if hb == 0 and have_prev:
                    emit_sq2(g - 1, 0)
                elif hb == 1 and have_prev:
                    emit_sq2(g - 1, 1)
                elif hb == 2 and have_prev:
                    st2 = ln_stats_pair(
                        g - 1, 1, [f"ln2s{g - 1}_0", f"ln2s{g - 1}_1"],
                        [f"sq2_{g - 1}_0", f"sq2_{g - 1}_1"])
                elif hb == 3 and have_prev:
                    emit_nrm2_out(g - 1, 0, st2[0], st2[1], st2[2], st2[3],
                                  f"nrm1_{g}_0")
                elif hb == 4 and have_prev:
                    emit_nrm2_out(g - 1, 1, st2[0], st2[1], st2[2], st2[3],
                                  f"nrm1_{g}_1")
                elif hb == 5 and have_next:
                    emit_AO(g + 1, 0, extra=[f"xev{g}_1_0", f"xev{g}_1_1"])
                    load_x(2 * g + 2)
                    load_x(2 * g + 3)
                elif hb == 7 and have_next:
                    emit_r1stt(g + 1, 0)
                    emit_sq1(g + 1, 0)
                elif hb == 8 and have_next:
                    emit_AO(g + 1, 1, extra=[f"r1_{g + 1}_0"])
                elif hb == 9 and have_next:
                    emit_r1stt(g + 1, 1)
                    emit_sq1(g + 1, 1)
                elif hb == 11 and have_next:
                    st1 = ln_stats_pair(
                        g + 1, 0, [f"r1_{g + 1}_0", f"r1_{g + 1}_1"],
                        [f"sq1_{g + 1}_0", f"sq1_{g + 1}_1"])
                elif hb == 13 and have_next:
                    emit_nrm1(g + 1, 0, st1[2], st1[3], st1[0], st1[1])
                    emit_nrm1(g + 1, 1, st1[2], st1[3], st1[0], st1[1])
                elif hb == 15 and g + 2 < ngroups:
                    load_xT(g + 2, [f"h1{g}"])

            emit_down(NHB - 1)

            if have_next:
                emit_xtp_xev(g + 1)
                emit_ln2s(g, 0)
                emit_ln2s(g, 1)
                if g + 2 < ngroups:
                    emit_H1(g + 2, extra=[f"xev{g + 1}_1_1"])
                    emit_gelu(g + 2)
                    emit_dummy(f"dmy_sig{g + 1}")

        gl = ngroups - 1
        emit_ln2s(gl, 0)
        emit_ln2s(gl, 1)
        emit_sq2(gl, 0)
        emit_sq2(gl, 1)
        st2 = ln_stats_pair(gl, 1, [f"ln2s{gl}_0", f"ln2s{gl}_1"],
                            [f"sq2_{gl}_0", f"sq2_{gl}_1"])
        emit_nrm2_out(gl, 0, st2[0], st2[1], st2[2], st2[3],
                      f"out{2 * gl - 2}")
        emit_nrm2_out(gl, 1, st2[0], st2[1], st2[2], st2[3],
                      f"out{2 * gl - 1}")

        p.ev["_all_soa"] = ("soa", p.cnt["soa"])
        p.ev["_all_sob"] = ("sob", p.cnt["sob"])
        p.ops["sy"].append((None, ["_all_soa", "_all_sob"], None, 0))

        with nc.Block() as block:
            @block.tensor
            def _(tensor):
                _run(p, "pe", tensor, sems)

            @block.vector
            def _(vector):
                _run(p, "dv", vector, sems)

            @block.scalar
            def _(scalar):
                _run(p, "sc", scalar, sems)

            @block.sync
            def _(sync):
                _run(p, "sy", sync, sems)

            @block.gpsimd
            def _(gpsimd):
                _run(p, "gp", gpsimd, sems)

    return nc


def _run(p, eng, engobj, sems):
    satisfied = {}
    for fn, waits, sem, n in p.ops[eng]:
        merged = {}
        for name in waits:
            sname, val = p.ev[name]
            if val > merged.get(sname, 0):
                merged[sname] = val
        for sname, val in merged.items():
            if satisfied.get(sname, 0) >= val:
                continue
            engobj.wait_ge(sems[sname], val)
            satisfied[sname] = val
        if fn is None:
            continue
        instr = fn(engobj)
        if sem is not None:
            instr.then_inc(sems[sem], n)


def kernel(L0_attn_x, L0_attn_W_K, L0_attn_W_V, L0_attn_W1, L0_attn_W2,
           L0_attn_m1, L0_attn_m2, L0_ffn_w_gate, L0_ffn_w_up, L0_ffn_w_down):
    x = np.ascontiguousarray(np.asarray(L0_attn_x, dtype=np.float32))
    B, S, _ = x.shape
    rows = S
    wk = np.ascontiguousarray(np.asarray(L0_attn_W_K, dtype=np.float32))
    w1T = np.ascontiguousarray(np.asarray(L0_attn_W1, dtype=np.float32).T)
    w2T = np.ascontiguousarray(np.asarray(L0_attn_W2, dtype=np.float32).T)
    wgT = np.ascontiguousarray(np.asarray(L0_ffn_w_gate, dtype=np.int32).T)
    wuT = np.ascontiguousarray(np.asarray(L0_ffn_w_up, dtype=np.int32).T)
    wdT = np.ascontiguousarray(np.asarray(L0_ffn_w_down, dtype=np.int32).T)
    idf = np.eye(128, dtype=np.float32)

    nc = build_core(rows)
    in_maps = []
    for i in range(B):
        xi = np.ascontiguousarray(x[i])
        in_maps.append({
            "x": xi,
            "xT": np.ascontiguousarray(xi.T),
            "wk": wk, "w1T": w1T, "w2T": w2T,
            "wgT": wgT, "wuT": wuT, "wdT": wdT,
            "idf": idf,
        })
    res = run_bass_kernel_spmd(nc, in_maps, core_ids=list(range(B)))
    out = np.stack([res.results[i]["out"] for i in range(B)], axis=0)
    return out.astype(np.float32)
